# version 22
# baseline (speedup 1.0000x reference)
import numpy as np
import ml_dtypes
from contextlib import ExitStack

import concourse.bass as bass
import concourse.mybir as mybir
from concourse.bass_utils import run_bass_kernel_spmd

F32 = mybir.dt.float32
BF16 = mybir.dt.bfloat16
AF = mybir.ActivationFunctionType
ALU = mybir.AluOpType
bfdt = ml_dtypes.bfloat16

D = 1024
T = 8192
S = 512
NST = T // S
NSUB = S // 128
DH = 4096
NL = 2
DEC = 0.95
EPS = 1e-5
NCORES = 8


class Em:
    ENG = ("tensor", "vector", "scalar", "gpsimd", "sync")

    def __init__(self, nc):
        self.nc = nc
        self.items = {e: [] for e in self.ENG}
        self.sems = {}
        self.cnt = {}
        for e in self.ENG:
            self._mk("m_" + e)
        self.q = {"qs": ["sync", 8, 0], "qp": ["gpsimd", 8, 0], "qa": ["scalar", 8, 0]}
        for qn, (eng, n, _) in self.q.items():
            for i in range(n):
                self._mk(f"{qn}{i}")
        self.waited = {e: {} for e in self.ENG}
        self.lastw = {}
        self.readers = {}
        self.floor = {}

    def _mk(self, name):
        self.sems[name] = self.nc.alloc_semaphore(name)
        self.cnt[name] = 0

    def _deps(self, eng, reads, writes, same_engine_ok=False):
        w = dict(self.floor)
        def add(tok):
            if tok is None:
                return
            s, v = tok
            if w.get(s, 0) < v:
                w[s] = v
        for k in reads:
            if k.startswith("scr:"):
                assert k in self.lastw, f"read of scratch {k} before any writer emitted"
            add(self.lastw.get(k))
        for k in writes:
            add(self.lastw.get(k))
            for s, v in self.readers.get(k, {}).items():
                if w.get(s, 0) < v:
                    w[s] = v
        own = "m_" + eng
        out = []
        wt = self.waited[eng]
        for s, v in w.items():
            if (same_engine_ok or eng == "tensor") and s == own:
                continue
            if wt.get(s, 0) < v:
                wt[s] = v
                out.append((s, v))
        return out

    def _mark(self, tok, reads, writes):
        for k in writes:
            self.lastw[k] = tok
            self.readers[k] = {}
        s, v = tok
        for k in reads:
            r = self.readers.setdefault(k, {})
            if r.get(s, 0) < v:
                r[s] = v

    def emit(self, eng, fn, reads=(), writes=(), same_engine_ok=False):
        waits = self._deps(eng, reads, writes, same_engine_ok)
        sem = "m_" + eng
        self.cnt[sem] += 1
        tok = (sem, self.cnt[sem])
        self.items[eng].append((waits, fn, sem, 1))
        self._mark(tok, reads, writes)
        return tok

    def dma(self, qn, fn, reads=(), writes=()):
        eng, n, cnt = self.q[qn]
        self.q[qn][2] = cnt + 1
        sem = f"{qn}{cnt % n}"
        waits = self._deps(eng, reads, writes)
        prev = self.cnt[sem]
        if prev > 0:
            wt = self.waited[eng]
            if wt.get(sem, 0) < prev:
                wt[sem] = prev
                waits.append((sem, prev))
        self.cnt[sem] += 16
        tok = (sem, self.cnt[sem])
        self.items[eng].append((waits, fn, sem, 16))
        self._mark(tok, reads, writes)
        return tok

    def barrier(self):
        self.floor = {s: c for s, c in self.cnt.items() if c > 0}

    def final_waits(self):
        def f(eng, sems=self.sems, cnt=dict(self.cnt)):
            for s, c in cnt.items():
                if c > 0 and not s.startswith("m_"):
                    eng.wait_ge(sems[s], c)
            return eng.nop()
        self.items["gpsimd"].append(("FINAL", f, None, 0))

    def realize(self, block):
        for e in self.ENG:
            its = self.items[e]
            if not its:
                continue
            def body(eng, its=its):
                for waits, fn, sem, inc in its:
                    if waits == "FINAL":
                        fn(eng)
                        continue
                    for s, v in waits:
                        eng.wait_ge(self.sems[s], v)
                    inst = fn(eng)
                    if inc:
                        inst.then_inc(self.sems[sem], inc)
            getattr(block, e)(body)


def build_program(debug=0):
    nc = bass.Bass("TRN2", target_bir_lowering=False)

    x_in = nc.dram_tensor("x_in", [T, D], F32, kind="ExternalInput")
    wk = nc.dram_tensor("wk", [NL, D, D], F32, kind="ExternalInput")
    wv = nc.dram_tensor("wv", [NL, D, D], F32, kind="ExternalInput")
    wr = nc.dram_tensor("wr", [NL, D, D], F32, kind="ExternalInput")
    wo = nc.dram_tensor("wo", [NL, D, D], F32, kind="ExternalInput")
    lnw_c = nc.dram_tensor("lnw_c", [NL, 128, 8], F32, kind="ExternalInput")
    lnb_c = nc.dram_tensor("lnb_c", [NL, 128, 8], F32, kind="ExternalInput")
    w1 = nc.dram_tensor("w1", [DH, D], F32, kind="ExternalInput")
    w2 = nc.dram_tensor("w2", [DH, D], F32, kind="ExternalInput")
    h1 = nc.dram_tensor("h1", [DH, DH], F32, kind="ExternalInput")
    h2 = nc.dram_tensor("h2", [DH, DH], F32, kind="ExternalInput")
    w3 = nc.dram_tensor("w3", [D, DH], F32, kind="ExternalInput")
    s1c = nc.dram_tensor("s1c", [128, 32], F32, kind="ExternalInput")
    s2c = nc.dram_tensor("s2c", [128, 32], F32, kind="ExternalInput")
    h1sc = nc.dram_tensor("h1sc", [128, 32], F32, kind="ExternalInput")
    h2sc = nc.dram_tensor("h2sc", [128, 32], F32, kind="ExternalInput")
    w3sc = nc.dram_tensor("w3sc", [128, 8], F32, kind="ExternalInput")
    ident_in = nc.dram_tensor("ident_in", [128, 128], BF16, kind="ExternalInput")

    y_out = nc.dram_tensor("y", [T, D], F32, kind="ExternalOutput")

    dbgk = "ExternalOutput" if debug in (1, 2) else None
    def scratch(name, shape, dtype, dbg=False):
        if dbg and dbgk:
            return nc.dram_tensor(name, shape, dtype, kind=dbgk)
        return nc.dram_tensor(name, shape, dtype)
    wkvrT = scratch("wkvrT", [NL, 8, 128, 3, 8, 128], BF16, dbg=True)
    woT = scratch("woT", [NL, 128, 8, D], BF16, dbg=True)
    w12T = scratch("w12T", [32, 128, 2, 8, 128], BF16)
    h1T = scratch("h1T", [32, 128, 32, 128], BF16)
    h2T = scratch("h2T", [32, 128, 32, 128], BF16)
    w3T = scratch("w3T", [8, 128, 32, 128], BF16)
    x2f = scratch("x2f", [T, D], F32, dbg=(debug == 2))
    x2cm = scratch("x2cm", [NST, 128, 8, S], BF16, dbg=(debug == 2))

    em = Em(nc)
    stack = ExitStack()
    en = stack.enter_context

    s_ident = en(nc.sbuf_tensor("s_ident", [128, 128], BF16))
    s_lnw = en(nc.sbuf_tensor("s_lnw", [128, NL * 8], F32))
    s_lnb = en(nc.sbuf_tensor("s_lnb", [128, NL * 8], F32))
    s_lnbb = en(nc.sbuf_tensor("s_lnbb", [128, NL * 8], BF16))
    s_bias = en(nc.sbuf_tensor("s_bias", [128, NL * 3 * 8], F32))
    s_s1 = en(nc.sbuf_tensor("s_s1", [128, 32], F32))
    s_s2 = en(nc.sbuf_tensor("s_s2", [128, 32], F32))
    s_h1s = en(nc.sbuf_tensor("s_h1s", [128, 32], F32))
    s_h2s = en(nc.sbuf_tensor("s_h2s", [128, 32], F32))
    s_w3s = en(nc.sbuf_tensor("s_w3s", [128, 8], F32))
    s_decay = en(nc.sbuf_tensor("s_decay", [128, S], F32))
    s_carry = en(nc.sbuf_tensor("s_carry", [128, NL * 8], BF16))
    s_stat = en(nc.sbuf_tensor("s_stat", [128, 4 * 12], F32))
    s_mv = en(nc.sbuf_tensor("s_mv", [128, 4 * 2], F32))
    s_rstd = en(nc.sbuf_tensor("s_rstd", [128, 4], F32))
    s_lnv = en(nc.sbuf_tensor("s_lnv", [128, 4], F32))
    s_nrstd = en(nc.sbuf_tensor("s_nrstd", [128, 4], F32))
    s_nmr = en(nc.sbuf_tensor("s_nmr", [128, 4], F32))
    s_eps = en(nc.sbuf_tensor("s_eps", [128, 1], F32))

    P_K = en(nc.psum_tensor("P_K", [128, 512], F32))
    P_V = en(nc.psum_tensor("P_V", [128, 512], F32))
    P_WO = en(nc.psum_tensor("P_WO", [128, 1024], F32))
    P_WO2 = en(nc.psum_tensor("P_WO2", [128, 1024], F32))
    P_T = en(nc.psum_tensor("P_T", [128, 1024], BF16))
    P_U = en(nc.psum_tensor("P_U", [128, 1024], BF16))

    em.dma("qs", lambda e: e.dma_start(out=s_ident[:], in_=ident_in.ap()), writes=["ident"])
    em.dma("qs", lambda e: e.dma_start(
        out=s_lnw[:].rearrange("p (l c) -> p l c", l=NL), in_=lnw_c.ap().rearrange("l p c -> p l c")),
        writes=["lnw"])
    em.dma("qs", lambda e: e.dma_start(
        out=s_lnb[:].rearrange("p (l c) -> p l c", l=NL), in_=lnb_c.ap().rearrange("l p c -> p l c")),
        writes=["lnb"])
    em.dma("qs", lambda e: e.dma_start(out=s_s1[:], in_=s1c.ap()), writes=["s1"])
    em.dma("qs", lambda e: e.dma_start(out=s_s2[:], in_=s2c.ap()), writes=["s2"])
    em.dma("qs", lambda e: e.dma_start(out=s_h1s[:], in_=h1sc.ap()), writes=["h1s"])
    em.dma("qs", lambda e: e.dma_start(out=s_h2s[:], in_=h2sc.ap()), writes=["h2s"])
    em.dma("qs", lambda e: e.dma_start(out=s_w3s[:], in_=w3sc.ap()), writes=["w3s"])
    em.emit("vector", lambda e: e.tensor_copy(s_lnbb[:], s_lnb[:]), reads=["lnb"], writes=["lnbb"])
    em.emit("vector", lambda e: e.memset(s_decay[:], DEC), writes=["decay"])
    em.emit("vector", lambda e: e.memset(s_eps[:], EPS), writes=["epsc"])
    em.emit("vector", lambda e: e.memset(s_carry[:], 0.0),
            writes=[f"carry:{l}:{c}" for l in range(NL) for c in range(8)])

    with ExitStack() as p1:
        pen = p1.enter_context
        pnl = [pen(nc.sbuf_tensor(f"pnl{i}", [128, 2048], BF16)) for i in range(2)]
        stg = [pen(nc.sbuf_tensor(f"stg{i}", [128, 2048], BF16)) for i in range(2)]
        s_x = pen(nc.sbuf_tensor("s_x", [128, 8 * 1024], F32))
        s_x0 = pen(nc.sbuf_tensor("s_x0", [128, 2 * 1024], F32))
        s_ncm = pen(nc.sbuf_tensor("s_ncm", [128, 2 * 8 * S], BF16))
        s_kv = pen(nc.sbuf_tensor("s_kv", [128, 8 * S], BF16))
        s_rr = pen(nc.sbuf_tensor("s_rr", [128, 8 * S], BF16))
        s_stt = pen(nc.sbuf_tensor("s_stt", [128, 8 * S], BF16))
        s_k05 = pen(nc.sbuf_tensor("s_k05", [128, 2 * S], BF16))
        s_vb = pen(nc.sbuf_tensor("s_vb", [128, 2 * S], BF16))
        s_xp = pen(nc.sbuf_tensor("s_xp", [128, 4 * 1024], BF16))
        s_wkvr = pen(nc.sbuf_tensor("s_wkvr", [128, 4 * 3 * 8 * 128], BF16))
        s_wo = pen(nc.sbuf_tensor("s_wo", [128, 8 * 1024], BF16))
        s_x2b = pen(nc.sbuf_tensor("s_x2b", [128, 4 * 1024], BF16))
        s_xcs = pen(nc.sbuf_tensor("s_xcs", [128, 8 * S], BF16))

        tasks = []

        def add_attn_tasks():
            wmap = [(0, wk, 0.05), (1, wv, 1.0), (2, wr, 1.0)]
            for L in range(NL):
                for wi, wt_, sc in wmap:
                    for oc in range(8):
                        def dst(e, stgbuf, L=L, wi=wi, oc=oc):
                            return e.dma_start(
                                out=wkvrT.ap()[L, oc, :, wi, :, :],
                                in_=stgbuf[:, 0:1024].rearrange("p (ic e) -> p ic e", ic=8))
                        tasks.append((wt_.ap()[L, oc * 128:(oc + 1) * 128, :], 8, dst,
                                      f"scr:kvr:{L}:{oc}:{wi}", (L, wi, sc, oc)))
                for oc in range(8):
                    def dsto(e, stgbuf, L=L, oc=oc):
                        return e.dma_start(
                            out=woT.ap()[L, :, :, oc * 128:(oc + 1) * 128],
                            in_=stgbuf[:, 0:1024].rearrange("p (c e) -> p c e", c=8))
                    tasks.append((wo.ap()[L, oc * 128:(oc + 1) * 128, :], 8, dsto,
                                  f"scr:wo:{L}:{oc}", None))

        def add_ffn_tasks():
            for wi, wt_ in ((0, w1), (1, w2)):
                for oc in range(32):
                    def dst12(e, stgbuf, wi=wi, oc=oc):
                        return e.dma_start(
                            out=w12T.ap()[oc, :, wi, :, :],
                            in_=stgbuf[:, 0:1024].rearrange("p (ic e) -> p ic e", ic=8))
                    tasks.append((wt_.ap()[oc * 128:(oc + 1) * 128, :], 8, dst12,
                                  f"scr:w12:{oc}:{wi}", None))
            for scr, wt_, no, nm in ((h1T, h1, 32, "h1"), (h2T, h2, 32, "h2"), (w3T, w3, 8, "w3")):
                for oc in range(no):
                    for half in range(2):
                        def dsth(e, stgbuf, scr=scr, oc=oc, half=half):
                            return e.dma_start(
                                out=scr.ap()[oc, :, half * 16:(half + 1) * 16, :],
                                in_=stgbuf[:, 0:2048].rearrange("p (ic e) -> p ic e", ic=16))
                        tasks.append((wt_.ap()[oc * 128:(oc + 1) * 128, half * 2048:(half + 1) * 2048],
                                      16, dsth, f"scr:{nm}:{oc}:{half}", None))

        pump_state = {"li": 0, "pi": 0}

        def pump_load():
            li = pump_state["li"]
            if li >= len(tasks) or li - pump_state["pi"] >= 2:
                return False
            src, nb, _, _, _ = tasks[li]
            i = li % 2
            em.dma("qp", lambda e, src=src, i=i, nb=nb: e.dma_start(
                out=pnl[i][:, 0:nb * 128], in_=src), writes=[f"pnl{i}"])
            pump_state["li"] = li + 1
            return True

        def pump_proc():
            pi = pump_state["pi"]
            if pi >= pump_state["li"]:
                return False
            src, nb, store_fn, skey, extra = tasks[pi]
            i = pi % 2
            kp, ks = f"pnl{i}", f"stg{i}"
            for g in range((nb + 7) // 8):
                lo, hi = g * 8, min(g * 8 + 8, nb)
                pt, ptk = (P_T, "P_T") if (pi + g) % 2 == 0 else (P_U, "P_U")
                def tr(e, i=i, lo=lo, hi=hi, pt=pt):
                    for c in range(lo, hi):
                        mm = e.matmul(pt[:, (c - lo) * 128:(c - lo + 1) * 128],
                                      pnl[i][:, c * 128:(c + 1) * 128], s_ident[:],
                                      is_transpose=True)
                    return mm
                em.emit("tensor", tr, reads=[kp, "ident"], writes=[ptk])
                em.emit("scalar", lambda e, i=i, lo=lo, hi=hi, pt=pt: e.copy(
                    stg[i][:, lo * 128:hi * 128], pt[:, 0:(hi - lo) * 128]),
                    reads=[ptk], writes=[ks])
            if extra is not None:
                L, wi, sc, oc = extra
                def bmm(e, i=i, L=L):
                    for ic in range(8):
                        mm = e.matmul(P_WO[:, 0:1], stg[i][:, ic * 128:(ic + 1) * 128],
                                      s_lnbb[:, L * 8 + ic:L * 8 + ic + 1],
                                      start=(ic == 0), stop=(ic == 7))
                    return mm
                em.emit("tensor", bmm, reads=[ks, "lnbb"], writes=["P_WO"])
                bslot = s_bias[:, (L * 3 + wi) * 8 + oc:(L * 3 + wi) * 8 + oc + 1]
                em.emit("scalar", lambda e, bslot=bslot, sc=sc: e.mul(bslot, P_WO[:, 0:1], sc),
                        reads=["P_WO"], writes=[f"bias:{L}:{wi}:{oc}"])
                for ic in range(8):
                    em.emit("vector", lambda e, i=i, ic=ic, L=L: e.tensor_scalar(
                        out=stg[i][:, ic * 128:(ic + 1) * 128], in0=stg[i][:, ic * 128:(ic + 1) * 128],
                        scalar1=s_lnw[:, L * 8 + ic:L * 8 + ic + 1], scalar2=None, op0=ALU.mult),
                        reads=[ks, "lnw"], writes=[ks])
            em.dma("qa", lambda e, i=i, store_fn=store_fn: store_fn(e, stg[i]), reads=[ks], writes=[skey])
            pump_state["pi"] = pi + 1
            return True

        def pump(n):
            for _ in range(n):
                a = pump_proc()
                b = pump_load()
                if not (a or b):
                    return

        add_attn_tasks()
        while pump_state["pi"] < 32:
            pump(1)
        add_ffn_tasks()
        pump(2)

        if debug == 1:
            while pump_state["pi"] < len(tasks):
                pump(1)
            em.final_waits()
            with nc.Block() as block:
                em.realize(block)
            return nc

        lncnt = [0]

        def ln_stats(xap, ukey):
            slot = lncnt[0] % 4
            lncnt[0] += 1
            st12 = s_stat[:, slot * 12:(slot + 1) * 12]
            mv = s_mv[:, slot * 2:(slot + 1) * 2]
            em.emit("vector", lambda e: e.bn_stats(st12[:, 0:6], xap[:, 0:512]),
                    reads=[ukey], writes=[f"stat{slot}"])
            em.emit("vector", lambda e: e.bn_stats(st12[:, 6:12], xap[:, 512:1024]),
                    reads=[ukey], writes=[f"stat{slot}"])
            em.emit("vector", lambda e: e.bn_aggr(mv, st12), reads=[f"stat{slot}"], writes=[f"mv{slot}"])
            em.emit("scalar", lambda e: e.activation(s_lnv[:, slot:slot + 1], mv[:, 1:2],
                                                     AF.Ln, bias=s_eps[:, 0:1], scale=1.0),
                    reads=[f"mv{slot}", "epsc"], writes=[f"lnv{slot}"])
            em.emit("scalar", lambda e: e.activation(s_rstd[:, slot:slot + 1], s_lnv[:, slot:slot + 1],
                                                     AF.Exp, scale=-0.5),
                    reads=[f"lnv{slot}"], writes=[f"rstd{slot}"])
            em.emit("scalar", lambda e: e.activation(s_nrstd[:, slot:slot + 1], s_rstd[:, slot:slot + 1],
                                                     AF.Identity, scale=-1.0),
                    reads=[f"rstd{slot}"], writes=[f"nrstd{slot}"])
            em.emit("scalar", lambda e: e.activation(s_nmr[:, slot:slot + 1], mv[:, 0:1],
                                                     AF.Identity, scale=s_nrstd[:, slot:slot + 1]),
                    reads=[f"mv{slot}", f"nrstd{slot}"], writes=[f"nmr{slot}"])
            return slot

        def ln_apply(out_ap, xap, slot, rkeys, wkeys):
            em.emit("scalar", lambda e: e.activation(
                out_ap, xap, AF.Identity,
                scale=s_rstd[:, slot:slot + 1], bias=s_nmr[:, slot:slot + 1]),
                reads=list(rkeys) + [f"nmr{slot}", f"rstd{slot}"], writes=list(wkeys))

        def stage_A_pre(st, L, i):
            gt = st * NSUB + i
            xs = gt % 8
            xap = s_x[:, xs * 1024:(xs + 1) * 1024]
            xkey = f"x:{xs}"
            slot = ln_stats(xap, xkey)
            xslot = i % 4
            xpap = s_xp[:, xslot * 1024:(xslot + 1) * 1024]
            ln_apply(xpap, xap, slot, [xkey], [f"xp{xslot}"])

        def stage_A_post(st, L, i):
            xslot = i % 4
            xpap = s_xp[:, xslot * 1024:(xslot + 1) * 1024]
            pt, ptk = (P_T, "P_T") if i % 2 == 0 else (P_U, "P_U")
            def trx(e, xpap=xpap, pt=pt):
                for c in range(8):
                    mm = e.matmul(pt[:, c * 128:(c + 1) * 128],
                                  xpap[:, c * 128:(c + 1) * 128], s_ident[:],
                                  is_transpose=True)
                return mm
            em.emit("tensor", trx, reads=[f"xp{xslot}", "ident"], writes=[ptk])
            ncm = s_ncm[:, (L % 2) * 8 * S:(L % 2 + 1) * 8 * S]
            em.emit("scalar", lambda e, i=i, ncm=ncm, pt=pt: e.copy(
                ncm.rearrange("p (c t) -> p c t", c=8)[:, :, i * 128:(i + 1) * 128],
                pt[:].rearrange("p (c t) -> p c t", c=8)),
                reads=[ptk], writes=[f"ncm{L % 2}:{c}" for c in range(8)])

        def stage_A_tile(st, L, i):
            stage_A_pre(st, L, i)
            stage_A_post(st, L, i)

        def stage_x_loads(st):
            for i in range(NSUB):
                gt = st * NSUB + i
                xs = gt % 8
                xap = s_x[:, xs * 1024:(xs + 1) * 1024]
                em.dma("qp", lambda e, gt=gt, xap=xap: e.dma_start(
                    out=xap, in_=x_in.ap()[gt * 128:(gt + 1) * 128, :]), writes=[f"x:{xs}"])

        stage_x_loads(0)
        stage_x_loads(1)
        for i in range(NSUB):
            stage_A_tile(0, 0, i)
        deferred = []

        for st in range(NST):
            for L in range(NL):
                pp = L % 2
                ncm = s_ncm[:, pp * 8 * S:(pp + 1) * 8 * S]
                nk = [f"ncm{pp}:{c}" for c in range(8)]
                for ec in range(8):
                    wb = (st * NL * 8 + L * 8 + ec) % 4
                    wkey = f"wkvr{wb}"
                    em.dma("qs", lambda e, L=L, ec=ec, wb=wb: e.dma_start(
                        out=s_wkvr[:, wb * 3072:(wb + 1) * 3072].rearrange("p (w ic e) -> p w ic e", w=3, ic=8),
                        in_=wkvrT.ap()[L, ec, :, :, :, :]),
                        reads=[f"scr:kvr:{L}:{ec}:{wi}" for wi in range(3)], writes=[wkey])
                    kb = s_bias[:, (L * 3 + 0) * 8 + ec:(L * 3 + 0) * 8 + ec + 1]
                    vb = s_bias[:, (L * 3 + 1) * 8 + ec:(L * 3 + 1) * 8 + ec + 1]
                    rb = s_bias[:, (L * 3 + 2) * 8 + ec:(L * 3 + 2) * 8 + ec + 1]
                    eb = ec % 2
                    k05 = s_k05[:, eb * S:(eb + 1) * S]
                    vbf = s_vb[:, eb * S:(eb + 1) * S]
                    evacs = [
                        lambda: em.emit("scalar", lambda e, k05=k05, kb=kb: e.activation(
                            k05, P_K[:, 0:512], AF.Identity, scale=0.05, bias=kb),
                            reads=["P_K", f"bias:{L}:0:{ec}"], writes=[f"k05{eb}"]),
                        lambda: em.emit("scalar", lambda e, vbf=vbf, vb=vb: e.activation(
                            vbf, P_V[:, 0:512], AF.Identity, scale=1.0, bias=vb),
                            reads=["P_V", f"bias:{L}:1:{ec}"], writes=[f"vb{eb}"]),
                        lambda: em.emit("scalar", lambda e, ec=ec, rb=rb: e.activation(
                            s_rr[:, ec * S:(ec + 1) * S], P_WO[:, 0:512], AF.Sigmoid, scale=1.0, bias=rb),
                            reads=["P_WO", f"bias:{L}:2:{ec}"], writes=[f"rr:{ec}"]),
                    ]
                    for wi, (pw, pk) in enumerate(((P_K, "P_K"), (P_V, "P_V"), (P_WO[:, 0:512], "P_WO"))):
                        def kmm(e, wb=wb, wi=wi, pw=pw, ncm=ncm):
                            for dc in range(8):
                                mm = e.matmul(pw if wi == 2 else pw[:, 0:512],
                                              s_wkvr[:, wb * 3072 + (wi * 8 + dc) * 128: wb * 3072 + (wi * 8 + dc + 1) * 128],
                                              ncm[:, dc * S:(dc + 1) * S],
                                              start=(dc == 0), stop=(dc == 7))
                            return mm
                        em.emit("tensor", kmm, reads=[wkey] + nk, writes=[pk])
                        evacs[wi]()
                    em.emit("vector", lambda e, ec=ec, k05=k05, vbf=vbf: e.tensor_tensor(
                        s_kv[:, ec * S:(ec + 1) * S], k05, vbf, ALU.mult),
                        reads=[f"k05{eb}", f"vb{eb}"], writes=[f"kv:{ec}"])
                    cc = s_carry[:, L * 8 + ec:L * 8 + ec + 1]
                    em.emit("vector", lambda e, ec=ec, cc=cc: e.tensor_tensor_scan(
                        out=s_stt[:, ec * S:(ec + 1) * S], data0=s_decay[:], data1=s_kv[:, ec * S:(ec + 1) * S],
                        initial=cc, op0=ALU.mult, op1=ALU.add),
                        reads=[f"kv:{ec}", f"carry:{L}:{ec}", "decay"], writes=[f"st:{ec}"])
                    em.emit("vector", lambda e, ec=ec, cc=cc: e.tensor_copy(
                        cc, s_stt[:, ec * S + S - 1:ec * S + S]),
                        reads=[f"st:{ec}"], writes=[f"carry:{L}:{ec}"])
                    em.emit("vector", lambda e, ec=ec: e.tensor_tensor(
                        s_kv[:, ec * S:(ec + 1) * S], s_rr[:, ec * S:(ec + 1) * S],
                        s_stt[:, ec * S:(ec + 1) * S], ALU.mult),
                        reads=[f"rr:{ec}", f"st:{ec}"], writes=[f"kv:{ec}"])
                    pump(3 if st == 0 else 1)
                em.dma("qs", lambda e, L=L: e.dma_start(
                    out=s_wo[:], in_=woT.ap()[L, :, :, :]),
                    reads=[f"scr:wo:{L}:{oc}" for oc in range(8)], writes=["wo"])
                def nextA_pre(i):
                    if L == 0:
                        stage_A_pre(st, 1, i)
                    elif st + 1 < NST:
                        stage_A_pre(st + 1, 0, i)
                def nextA_post(i):
                    if L == 0:
                        stage_A_post(st, 1, i)
                    elif st + 1 < NST:
                        stage_A_post(st + 1, 0, i)
                for i in range(NSUB):
                    gt = st * NSUB + i
                    xs = gt % 8
                    xap = s_x[:, xs * 1024:(xs + 1) * 1024]
                    pwo, pwok = (P_WO, "P_WO") if i % 2 == 0 else (P_WO2, "P_WO2")
                    for c in range(8):
                        def wom(e, i=i, c=c, pwo=pwo):
                            for dh in range(2):
                                mm = e.matmul(pwo[:, dh * 512:(dh + 1) * 512],
                                              s_kv[:, c * S + i * 128:c * S + (i + 1) * 128],
                                              s_wo[:, c * 1024 + dh * 512:c * 1024 + (dh + 1) * 512],
                                              start=(c == 0), stop=(c == 7))
                            return mm
                        em.emit("tensor", wom, reads=[f"kv:{c}", "wo"], writes=[pwok],
                                same_engine_ok=(c > 0))
                    em.emit("vector", lambda e, xap=xap, pwo=pwo: e.tensor_tensor(
                        xap, xap, pwo[:, 0:1024], ALU.add),
                        reads=[pwok, f"x:{xs}"], writes=[f"x:{xs}"])
                    nextA_pre(i)
                    if L == 0:
                        for _ in range(2):
                            if deferred:
                                deferred.pop(0)()
                    if i >= 2:
                        nextA_post(i - 2)
                nextA_post(NSUB - 2)
                nextA_post(NSUB - 1)
            def make_postln(st):
                def chain(i):
                    gt = st * NSUB + i
                    xs = gt % 8
                    xslot = i % 2
                    bslot = i % 4
                    xap = s_x[:, xs * 1024:(xs + 1) * 1024]
                    x0ap = s_x0[:, xslot * 1024:(xslot + 1) * 1024]
                    em.dma("qp", lambda e, gt=gt, x0ap=x0ap: e.dma_start(
                        out=x0ap, in_=x_in.ap()[gt * 128:(gt + 1) * 128, :]), writes=[f"x0{xslot}"])
                    em.emit("vector", lambda e, xap=xap, x0ap=x0ap: e.tensor_tensor(
                        xap, xap, x0ap, ALU.add), reads=[f"x:{xs}", f"x0{xslot}"], writes=[f"x:{xs}"])
                    slot = ln_stats(xap, f"x:{xs}")
                    ln_apply(xap, xap, slot, [f"x:{xs}"], [f"x:{xs}"])
                    em.dma("qa", lambda e, gt=gt, xap=xap: e.dma_start(
                        out=x2f.ap()[gt * 128:(gt + 1) * 128, :], in_=xap),
                        reads=[f"x:{xs}"], writes=[f"scr:x2f:{gt}"])
                    x2b = s_x2b[:, bslot * 1024:(bslot + 1) * 1024]
                    em.emit("vector", lambda e, xap=xap, x2b=x2b: e.tensor_copy(x2b, xap),
                            reads=[f"x:{xs}"], writes=[f"x2b{bslot}"])
                def tpart(i):
                    bslot = i % 4
                    x2b = s_x2b[:, bslot * 1024:(bslot + 1) * 1024]
                    pt, ptk = (P_T, "P_T") if i % 2 == 0 else (P_U, "P_U")
                    def trx2(e, x2b=x2b, pt=pt):
                        for c in range(8):
                            mm = e.matmul(pt[:, c * 128:(c + 1) * 128],
                                          x2b[:, c * 128:(c + 1) * 128], s_ident[:], is_transpose=True)
                        return mm
                    em.emit("tensor", trx2, reads=[f"x2b{bslot}", "ident"], writes=[ptk])
                    em.emit("scalar", lambda e, i=i, pt=pt: e.copy(
                        s_xcs[:].rearrange("p (c t) -> p c t", c=8)[:, :, i * 128:(i + 1) * 128],
                        pt[:].rearrange("p (c t) -> p c t", c=8)),
                        reads=[ptk], writes=["xcs"])
                def fin():
                    em.dma("qa", lambda e, st=st: e.dma_start(
                        out=x2cm.ap()[st, :, :, :],
                        in_=s_xcs[:].rearrange("p (c t) -> p c t", c=8)),
                        reads=["xcs"], writes=[f"scr:x2cm:{st}"])
                return [lambda: chain(0), lambda: chain(1),
                        lambda: (chain(2), tpart(0)), lambda: (chain(3), tpart(1)),
                        lambda: tpart(2), lambda: tpart(3), fin]
            deferred.extend(make_postln(st))
            if st + 2 < NST:
                deferred.append(lambda st=st: stage_x_loads(st + 2))
            if st == NST - 1:
                while deferred:
                    deferred.pop(0)()
            pump(2)
        while pump_state["pi"] < len(tasks):
            pump(1)
        em.barrier()

        if debug == 2:
            em.final_waits()
            with nc.Block() as block:
                em.realize(block)
            return nc

    with ExitStack() as p2:
        pen = p2.enter_context
        s_xcm = pen(nc.sbuf_tensor("s_xcm", [128, 2 * 8 * S], BF16))
        s_h = pen(nc.sbuf_tensor("s_h", [128, 32 * S], BF16))
        s_hh = pen(nc.sbuf_tensor("s_hh", [128, 32 * S], BF16))
        s_w12 = pen(nc.sbuf_tensor("s_w12", [128, 2 * 2048], BF16))
        s_hw = pen(nc.sbuf_tensor("s_hw", [128, 3 * 4096], BF16))
        s_sa = pen(nc.sbuf_tensor("s_sa", [128, 2 * S], BF16))
        s_sb = pen(nc.sbuf_tensor("s_sb", [128, 2 * S], BF16))
        s_fcm = pen(nc.sbuf_tensor("s_fcm", [128, 8 * S], BF16))
        s_x2r = pen(nc.sbuf_tensor("s_x2r", [128, 2 * 1024], F32))
        s_u = pen(nc.sbuf_tensor("s_u", [128, 2 * 1024], F32))
        s_outb = pen(nc.sbuf_tensor("s_outb", [128, 2 * 1024], F32))

        psl = [(P_K[:, 0:512], "P_K"), (P_V[:, 0:512], "P_V"),
               (P_WO[:, 0:512], "P_WOa"), (P_WO[:, 512:1024], "P_WOb"),
               (P_WO2[:, 0:512], "P_WO2a"), (P_WO2[:, 512:1024], "P_WO2b")]
        pscnt = [0]

        def pslot():
            p, k = psl[pscnt[0] % 6]
            pscnt[0] += 1
            return p, k

        lncnt2 = [0]

        def ln_stats2(xap, ukey):
            slot = lncnt2[0] % 4
            lncnt2[0] += 1
            st12 = s_stat[:, slot * 12:(slot + 1) * 12]
            mv = s_mv[:, slot * 2:(slot + 1) * 2]
            em.emit("vector", lambda e: e.bn_stats(st12[:, 0:6], xap[:, 0:512]),
                    reads=[ukey], writes=[f"stat{slot}"])
            em.emit("vector", lambda e: e.bn_stats(st12[:, 6:12], xap[:, 512:1024]),
                    reads=[ukey], writes=[f"stat{slot}"])
            em.emit("vector", lambda e: e.bn_aggr(mv, st12), reads=[f"stat{slot}"], writes=[f"mv{slot}"])
            em.emit("scalar", lambda e: e.activation(s_lnv[:, slot:slot + 1], mv[:, 1:2],
                                                     AF.Ln, bias=s_eps[:, 0:1], scale=1.0),
                    reads=[f"mv{slot}", "epsc"], writes=[f"lnv{slot}"])
            em.emit("scalar", lambda e: e.activation(s_rstd[:, slot:slot + 1], s_lnv[:, slot:slot + 1],
                                                     AF.Exp, scale=-0.5),
                    reads=[f"lnv{slot}"], writes=[f"rstd{slot}"])
            em.emit("scalar", lambda e: e.activation(s_nrstd[:, slot:slot + 1], s_rstd[:, slot:slot + 1],
                                                     AF.Identity, scale=-1.0),
                    reads=[f"rstd{slot}"], writes=[f"nrstd{slot}"])
            em.emit("scalar", lambda e: e.activation(s_nmr[:, slot:slot + 1], mv[:, 0:1],
                                                     AF.Identity, scale=s_nrstd[:, slot:slot + 1]),
                    reads=[f"mv{slot}", f"nrstd{slot}"], writes=[f"nmr{slot}"])
            return slot

        def ln_apply2(out_ap, xap, slot, rkeys, wkeys):
            em.emit("scalar", lambda e: e.activation(
                out_ap, xap, AF.Identity,
                scale=s_rstd[:, slot:slot + 1], bias=s_nmr[:, slot:slot + 1]),
                reads=list(rkeys) + [f"nmr{slot}", f"rstd{slot}"], writes=list(wkeys))

        for st in range(NST):
            xb = st % 2
            xcmk = f"xcm{xb}"
            xcm = s_xcm[:, xb * 8 * S:(xb + 1) * 8 * S]
            em.dma("qp", lambda e, st=st, xcm=xcm: e.dma_start(
                out=xcm.rearrange("p (c t) -> p c t", c=8),
                in_=x2cm.ap()[st, :, :, :]),
                reads=[f"scr:x2cm:{st}"], writes=[xcmk])
            for oc in range(32):
                wb = oc % 2
                em.dma("qs", lambda e, oc=oc, wb=wb: e.dma_start(
                    out=s_w12[:, wb * 2048:(wb + 1) * 2048].rearrange("p (w ic e) -> p w ic e", w=2, ic=8),
                    in_=w12T.ap()[oc, :, :, :, :]),
                    reads=[f"scr:w12:{oc}:0", f"scr:w12:{oc}:1"], writes=[f"w12{wb}"])
                pa, pak = pslot()
                pb, pbk = pslot()
                def mma(e, wb=wb, pa=pa, xcm=xcm):
                    for ic in range(8):
                        mm = e.matmul(pa, s_w12[:, wb * 2048 + ic * 128:wb * 2048 + (ic + 1) * 128],
                                      xcm[:, ic * S:(ic + 1) * S], start=(ic == 0), stop=(ic == 7))
                    return mm
                def mmb(e, wb=wb, pb=pb, xcm=xcm):
                    for ic in range(8):
                        mm = e.matmul(pb, s_w12[:, wb * 2048 + (8 + ic) * 128:wb * 2048 + (9 + ic) * 128],
                                      xcm[:, ic * S:(ic + 1) * S], start=(ic == 0), stop=(ic == 7))
                    return mm
                def mmab(e, mma=mma, mmb=mmb):
                    mma(e)
                    return mmb(e)
                em.emit("tensor", mmab, reads=[f"w12{wb}", xcmk], writes=[pak, pbk])
                sa = s_sa[:, wb * S:(wb + 1) * S]
                sb = s_sb[:, wb * S:(wb + 1) * S]
                em.emit("scalar", lambda e, sa=sa, pa=pa, oc=oc: e.activation(
                    sa, pa, AF.Silu, scale=s_s1[:, oc:oc + 1]),
                    reads=[pak, "s1"], writes=[f"sa{wb}"])
                em.emit("scalar", lambda e, sb=sb, pb=pb, oc=oc: e.activation(
                    sb, pb, AF.Identity, scale=s_s2[:, oc:oc + 1]),
                    reads=[pbk, "s2"], writes=[f"sb{wb}"])
                em.emit("vector", lambda e, oc=oc, sa=sa, sb=sb: e.tensor_tensor(
                    s_h[:, oc * S:(oc + 1) * S], sa, sb, ALU.mult),
                    reads=[f"sa{wb}", f"sb{wb}"], writes=[f"h:{oc}"])
            hkeys = [f"h:{c}" for c in range(32)]
            for oc in range(32):
                wb = oc % 3
                em.dma("qs", lambda e, oc=oc, wb=wb: e.dma_start(
                    out=s_hw[:, wb * 4096:(wb + 1) * 4096].rearrange("p (ic e) -> p ic e", ic=32),
                    in_=h1T.ap()[oc, :, :, :]),
                    reads=[f"scr:h1:{oc}:0", f"scr:h1:{oc}:1"], writes=[f"hw{wb}"])
                pa, pak = pslot()
                def hmm(e, wb=wb, pa=pa):
                    for ic in range(32):
                        mm = e.matmul(pa, s_hw[:, wb * 4096 + ic * 128:wb * 4096 + (ic + 1) * 128],
                                      s_h[:, ic * S:(ic + 1) * S], start=(ic == 0), stop=(ic == 31))
                    return mm
                em.emit("tensor", hmm, reads=[f"hw{wb}"] + hkeys, writes=[pak])
                em.emit("scalar", lambda e, oc=oc, pa=pa: e.activation(
                    s_hh[:, oc * S:(oc + 1) * S], pa, AF.Gelu, scale=s_h1s[:, oc:oc + 1]),
                    reads=[pak, "h1s"], writes=[f"hh:{oc}"])
            hhkeys = [f"hh:{c}" for c in range(32)]
            for oc in range(32):
                wb = oc % 3
                em.dma("qs", lambda e, oc=oc, wb=wb: e.dma_start(
                    out=s_hw[:, wb * 4096:(wb + 1) * 4096].rearrange("p (ic e) -> p ic e", ic=32),
                    in_=h2T.ap()[oc, :, :, :]),
                    reads=[f"scr:h2:{oc}:0", f"scr:h2:{oc}:1"], writes=[f"hw{wb}"])
                pa, pak = pslot()
                def h2mm(e, wb=wb, pa=pa):
                    for ic in range(32):
                        mm = e.matmul(pa, s_hw[:, wb * 4096 + ic * 128:wb * 4096 + (ic + 1) * 128],
                                      s_hh[:, ic * S:(ic + 1) * S], start=(ic == 0), stop=(ic == 31))
                    return mm
                em.emit("tensor", h2mm, reads=[f"hw{wb}"] + hhkeys, writes=[pak])
                em.emit("scalar", lambda e, oc=oc, pa=pa: e.activation(
                    s_h[:, oc * S:(oc + 1) * S], pa, AF.Gelu, scale=s_h2s[:, oc:oc + 1]),
                    reads=[pak, "h2s"], writes=[f"h:{oc}"])
            for oc in range(8):
                wb = oc % 3
                em.dma("qs", lambda e, oc=oc, wb=wb: e.dma_start(
                    out=s_hw[:, wb * 4096:(wb + 1) * 4096].rearrange("p (ic e) -> p ic e", ic=32),
                    in_=w3T.ap()[oc, :, :, :]),
                    reads=[f"scr:w3:{oc}:0", f"scr:w3:{oc}:1"], writes=[f"hw{wb}"])
                pa, pak = pslot()
                def w3mm(e, wb=wb, pa=pa):
                    for ic in range(32):
                        mm = e.matmul(pa, s_hw[:, wb * 4096 + ic * 128:wb * 4096 + (ic + 1) * 128],
                                      s_h[:, ic * S:(ic + 1) * S], start=(ic == 0), stop=(ic == 31))
                    return mm
                em.emit("tensor", w3mm, reads=[f"hw{wb}"] + hkeys, writes=[pak])
                em.emit("scalar", lambda e, oc=oc, pa=pa: e.activation(
                    s_fcm[:, oc * S:(oc + 1) * S], pa, AF.Identity, scale=s_w3s[:, oc:oc + 1]),
                    reads=[pak, "w3s"], writes=[f"fcm:{oc}"])
            for i in range(NSUB):
                gt = st * NSUB + i
                slot = i % 2
                x2r = s_x2r[:, slot * 1024:(slot + 1) * 1024]
                em.dma("qp", lambda e, gt=gt, x2r=x2r: e.dma_start(
                    out=x2r, in_=x2f.ap()[gt * 128:(gt + 1) * 128, :]),
                    reads=[f"scr:x2f:{gt}"], writes=[f"x2r{slot}"])
                pt, ptk = (P_T, "P_T") if i % 2 == 0 else (P_U, "P_U")
                def trf(e, i=i, pt=pt):
                    for c in range(8):
                        mm = e.matmul(pt[:, c * 128:(c + 1) * 128],
                                      s_fcm[:, c * S + i * 128:c * S + (i + 1) * 128],
                                      s_ident[:], is_transpose=True)
                    return mm
                em.emit("tensor", trf, reads=[f"fcm:{c}" for c in range(8)] + ["ident"], writes=[ptk])
                uap = s_u[:, slot * 1024:(slot + 1) * 1024]
                em.emit("vector", lambda e, uap=uap, x2r=x2r, pt=pt: e.tensor_tensor(
                    uap, x2r, pt[:], ALU.add),
                    reads=[ptk, f"x2r{slot}"], writes=[f"u{slot}"])
                lslot = ln_stats2(uap, f"u{slot}")
                ob = s_outb[:, slot * 1024:(slot + 1) * 1024]
                ln_apply2(ob, uap, lslot, [f"u{slot}"], [f"ob{slot}"])
                em.dma("qa", lambda e, gt=gt, ob=ob: e.dma_start(
                    out=y_out.ap()[gt * 128:(gt + 1) * 128, :], in_=ob), reads=[f"ob{slot}"])

    em.final_waits()
    with nc.Block() as block:
        em.realize(block)
    stack.close()
    return nc


def kernel(L0_attn_h, L0_attn_ln_w, L0_attn_ln_b, L0_attn_Wk, L0_attn_Wv, L0_attn_Wr,
           L0_attn_Wo, L0_ffn_w1_w, L0_ffn_w1_s, L0_ffn_w2_w, L0_ffn_w2_s,
           L0_ffn_h1_w, L0_ffn_h1_s, L0_ffn_h2_w, L0_ffn_h2_s, L0_ffn_w3_w, L0_ffn_w3_s):
    x = np.ascontiguousarray(np.asarray(L0_attn_h, dtype=np.float32))
    f32 = np.float32
    c = np.ascontiguousarray
    lnw_cv = c(np.asarray(L0_attn_ln_w, f32).reshape(NL, 8, 128).transpose(0, 2, 1))
    lnb_cv = c(np.asarray(L0_attn_ln_b, f32).reshape(NL, 8, 128).transpose(0, 2, 1))
    s1cv = c(np.asarray(L0_ffn_w1_s, f32).reshape(32, 128).T)
    s2cv = c(np.asarray(L0_ffn_w2_s, f32).reshape(32, 128).T)
    h1scv = c(np.asarray(L0_ffn_h1_s, f32).reshape(32, 128).T)
    h2scv = c(np.asarray(L0_ffn_h2_s, f32).reshape(32, 128).T)
    w3scv = c(np.asarray(L0_ffn_w3_s, f32).reshape(8, 128).T)
    ident = np.eye(128, dtype=np.float32).astype(bfdt)

    base = dict(
        wk=c(np.asarray(L0_attn_Wk, f32)), wv=c(np.asarray(L0_attn_Wv, f32)),
        wr=c(np.asarray(L0_attn_Wr, f32)), wo=c(np.asarray(L0_attn_Wo, f32)),
        lnw_c=lnw_cv, lnb_c=lnb_cv,
        w1=c(np.asarray(L0_ffn_w1_w, f32)), w2=c(np.asarray(L0_ffn_w2_w, f32)),
        h1=c(np.asarray(L0_ffn_h1_w, f32)), h2=c(np.asarray(L0_ffn_h2_w, f32)),
        w3=c(np.asarray(L0_ffn_w3_w, f32)),
        s1c=s1cv, s2c=s2cv, h1sc=h1scv, h2sc=h2scv, w3sc=w3scv, ident_in=ident,
    )
    in_maps = []
    for b in range(NCORES):
        m = dict(base)
        m["x_in"] = c(x[b])
        in_maps.append(m)

    nc = build_program()
    res = run_bass_kernel_spmd(nc, in_maps, core_ids=list(range(NCORES)))
    out = np.stack([np.asarray(res.results[b]["y"], dtype=np.float32) for b in range(NCORES)], axis=0)
    return out


# version 23
# speedup vs baseline: 1.1879x; 1.1879x over previous
import numpy as np
import ml_dtypes
from contextlib import ExitStack

import concourse.bass as bass
import concourse.mybir as mybir
from concourse.bass_utils import run_bass_kernel_spmd

F32 = mybir.dt.float32
BF16 = mybir.dt.bfloat16
AF = mybir.ActivationFunctionType
ALU = mybir.AluOpType
bfdt = ml_dtypes.bfloat16

D = 1024
T = 8192
S = 512
NST = T // S
NSUB = S // 128
DH = 4096
NL = 2
DEC = 0.95
EPS = 1e-5
NCORES = 8


class Em:
    ENG = ("tensor", "vector", "scalar", "gpsimd", "sync")

    def __init__(self, nc):
        self.nc = nc
        self.items = {e: [] for e in self.ENG}
        self.sems = {}
        self.cnt = {}
        for e in self.ENG:
            self._mk("m_" + e)
        self.q = {"qs": ["sync", 8, 0], "qp": ["gpsimd", 8, 0], "qa": ["scalar", 8, 0]}
        for qn, (eng, n, _) in self.q.items():
            for i in range(n):
                self._mk(f"{qn}{i}")
        self.waited = {e: {} for e in self.ENG}
        self.lastw = {}
        self.readers = {}
        self.floor = {}

    def _mk(self, name):
        self.sems[name] = self.nc.alloc_semaphore(name)
        self.cnt[name] = 0

    def _deps(self, eng, reads, writes, same_engine_ok=False):
        w = dict(self.floor)
        def add(tok):
            if tok is None:
                return
            s, v = tok
            if w.get(s, 0) < v:
                w[s] = v
        for k in reads:
            if k.startswith("scr:"):
                assert k in self.lastw, f"read of scratch {k} before any writer emitted"
            add(self.lastw.get(k))
        for k in writes:
            add(self.lastw.get(k))
            for s, v in self.readers.get(k, {}).items():
                if w.get(s, 0) < v:
                    w[s] = v
        own = "m_" + eng
        out = []
        wt = self.waited[eng]
        for s, v in w.items():
            if (same_engine_ok or eng == "tensor") and s == own:
                continue
            if wt.get(s, 0) < v:
                wt[s] = v
                out.append((s, v))
        return out

    def _mark(self, tok, reads, writes):
        for k in writes:
            self.lastw[k] = tok
            self.readers[k] = {}
        s, v = tok
        for k in reads:
            r = self.readers.setdefault(k, {})
            if r.get(s, 0) < v:
                r[s] = v

    def emit(self, eng, fn, reads=(), writes=(), same_engine_ok=False):
        waits = self._deps(eng, reads, writes, same_engine_ok)
        sem = "m_" + eng
        self.cnt[sem] += 1
        tok = (sem, self.cnt[sem])
        self.items[eng].append((waits, fn, sem, 1))
        self._mark(tok, reads, writes)
        return tok

    def dma(self, qn, fn, reads=(), writes=()):
        eng, n, cnt = self.q[qn]
        self.q[qn][2] = cnt + 1
        sem = f"{qn}{cnt % n}"
        waits = self._deps(eng, reads, writes)
        prev = self.cnt[sem]
        if prev > 0:
            wt = self.waited[eng]
            if wt.get(sem, 0) < prev:
                wt[sem] = prev
                waits.append((sem, prev))
        self.cnt[sem] += 16
        tok = (sem, self.cnt[sem])
        self.items[eng].append((waits, fn, sem, 16))
        self._mark(tok, reads, writes)
        return tok

    def barrier(self):
        self.floor = {s: c for s, c in self.cnt.items() if c > 0}

    def final_waits(self):
        def f(eng, sems=self.sems, cnt=dict(self.cnt)):
            for s, c in cnt.items():
                if c > 0 and not s.startswith("m_"):
                    eng.wait_ge(sems[s], c)
            return eng.nop()
        self.items["gpsimd"].append(("FINAL", f, None, 0))

    def realize(self, block):
        for e in self.ENG:
            its = self.items[e]
            if not its:
                continue
            def body(eng, its=its):
                for waits, fn, sem, inc in its:
                    if waits == "FINAL":
                        fn(eng)
                        continue
                    for s, v in waits:
                        eng.wait_ge(self.sems[s], v)
                    inst = fn(eng)
                    if inc:
                        inst.then_inc(self.sems[sem], inc)
            getattr(block, e)(body)


def build_program(debug=0):
    nc = bass.Bass("TRN2", target_bir_lowering=False)

    x_in = nc.dram_tensor("x_in", [T, D], F32, kind="ExternalInput")
    wk = nc.dram_tensor("wk", [NL, D, D], F32, kind="ExternalInput")
    wv = nc.dram_tensor("wv", [NL, D, D], F32, kind="ExternalInput")
    wr = nc.dram_tensor("wr", [NL, D, D], F32, kind="ExternalInput")
    wo = nc.dram_tensor("wo", [NL, D, D], F32, kind="ExternalInput")
    lnw_c = nc.dram_tensor("lnw_c", [NL, 128, 8], F32, kind="ExternalInput")
    lnb_c = nc.dram_tensor("lnb_c", [NL, 128, 8], F32, kind="ExternalInput")
    w1 = nc.dram_tensor("w1", [DH, D], F32, kind="ExternalInput")
    w2 = nc.dram_tensor("w2", [DH, D], F32, kind="ExternalInput")
    h1 = nc.dram_tensor("h1", [DH, DH], F32, kind="ExternalInput")
    h2 = nc.dram_tensor("h2", [DH, DH], F32, kind="ExternalInput")
    w3 = nc.dram_tensor("w3", [D, DH], F32, kind="ExternalInput")
    s1c = nc.dram_tensor("s1c", [128, 32], F32, kind="ExternalInput")
    s2c = nc.dram_tensor("s2c", [128, 32], F32, kind="ExternalInput")
    h1sc = nc.dram_tensor("h1sc", [128, 32], F32, kind="ExternalInput")
    h2sc = nc.dram_tensor("h2sc", [128, 32], F32, kind="ExternalInput")
    w3sc = nc.dram_tensor("w3sc", [128, 8], F32, kind="ExternalInput")
    ident_in = nc.dram_tensor("ident_in", [128, 128], BF16, kind="ExternalInput")

    y_out = nc.dram_tensor("y", [T, D], F32, kind="ExternalOutput")

    dbgk = "ExternalOutput" if debug in (1, 2) else None
    def scratch(name, shape, dtype, dbg=False):
        if dbg and dbgk:
            return nc.dram_tensor(name, shape, dtype, kind=dbgk)
        return nc.dram_tensor(name, shape, dtype)
    wkvrT = scratch("wkvrT", [NL, 8, 128, 3, 8, 128], BF16, dbg=True)
    woT = scratch("woT", [NL, 128, 8, D], BF16, dbg=True)
    w12T = scratch("w12T", [32, 128, 2, 8, 128], BF16)
    h1T = scratch("h1T", [32, 128, 32, 128], BF16)
    h2T = scratch("h2T", [32, 128, 32, 128], BF16)
    w3T = scratch("w3T", [8, 128, 32, 128], BF16)
    x2f = scratch("x2f", [T, D], F32, dbg=(debug == 2))
    x2cm = scratch("x2cm", [NST, 128, 8, S], BF16, dbg=(debug == 2))

    em = Em(nc)
    stack = ExitStack()
    en = stack.enter_context

    s_ident = en(nc.sbuf_tensor("s_ident", [128, 128], BF16))
    s_lnw = en(nc.sbuf_tensor("s_lnw", [128, NL * 8], F32))
    s_lnb = en(nc.sbuf_tensor("s_lnb", [128, NL * 8], F32))
    s_lnbb = en(nc.sbuf_tensor("s_lnbb", [128, NL * 8], BF16))
    s_bias = en(nc.sbuf_tensor("s_bias", [128, NL * 3 * 8], F32))
    s_s1 = en(nc.sbuf_tensor("s_s1", [128, 32], F32))
    s_s2 = en(nc.sbuf_tensor("s_s2", [128, 32], F32))
    s_h1s = en(nc.sbuf_tensor("s_h1s", [128, 32], F32))
    s_h2s = en(nc.sbuf_tensor("s_h2s", [128, 32], F32))
    s_w3s = en(nc.sbuf_tensor("s_w3s", [128, 8], F32))
    s_decay = en(nc.sbuf_tensor("s_decay", [128, S], F32))
    s_carry = en(nc.sbuf_tensor("s_carry", [128, NL * 8], BF16))
    s_stat = en(nc.sbuf_tensor("s_stat", [128, 4 * 12], F32))
    s_mv = en(nc.sbuf_tensor("s_mv", [128, 4 * 2], F32))
    s_rstd = en(nc.sbuf_tensor("s_rstd", [128, 4], F32))
    s_lnv = en(nc.sbuf_tensor("s_lnv", [128, 4], F32))
    s_nrstd = en(nc.sbuf_tensor("s_nrstd", [128, 4], F32))
    s_nmr = en(nc.sbuf_tensor("s_nmr", [128, 4], F32))
    s_eps = en(nc.sbuf_tensor("s_eps", [128, 1], F32))

    P_K = en(nc.psum_tensor("P_K", [128, 512], F32))
    P_V = en(nc.psum_tensor("P_V", [128, 512], F32))
    P_WO = en(nc.psum_tensor("P_WO", [128, 1024], F32))
    P_WO2 = en(nc.psum_tensor("P_WO2", [128, 1024], F32))
    P_T = en(nc.psum_tensor("P_T", [128, 1024], BF16))
    P_U = en(nc.psum_tensor("P_U", [128, 1024], BF16))

    em.dma("qs", lambda e: e.dma_start(out=s_ident[:], in_=ident_in.ap()), writes=["ident"])
    em.dma("qs", lambda e: e.dma_start(
        out=s_lnw[:].rearrange("p (l c) -> p l c", l=NL), in_=lnw_c.ap().rearrange("l p c -> p l c")),
        writes=["lnw"])
    em.dma("qs", lambda e: e.dma_start(
        out=s_lnb[:].rearrange("p (l c) -> p l c", l=NL), in_=lnb_c.ap().rearrange("l p c -> p l c")),
        writes=["lnb"])
    em.dma("qs", lambda e: e.dma_start(out=s_s1[:], in_=s1c.ap()), writes=["s1"])
    em.dma("qs", lambda e: e.dma_start(out=s_s2[:], in_=s2c.ap()), writes=["s2"])
    em.dma("qs", lambda e: e.dma_start(out=s_h1s[:], in_=h1sc.ap()), writes=["h1s"])
    em.dma("qs", lambda e: e.dma_start(out=s_h2s[:], in_=h2sc.ap()), writes=["h2s"])
    em.dma("qs", lambda e: e.dma_start(out=s_w3s[:], in_=w3sc.ap()), writes=["w3s"])
    em.emit("vector", lambda e: e.tensor_copy(s_lnbb[:], s_lnb[:]), reads=["lnb"], writes=["lnbb"])
    em.emit("vector", lambda e: e.memset(s_decay[:], DEC), writes=["decay"])
    em.emit("vector", lambda e: e.memset(s_eps[:], EPS), writes=["epsc"])
    em.emit("vector", lambda e: e.memset(s_carry[:], 0.0),
            writes=[f"carry:{l}:{c}" for l in range(NL) for c in range(8)])

    with ExitStack() as p1:
        pen = p1.enter_context
        pnl = [pen(nc.sbuf_tensor(f"pnl{i}", [128, 2048], BF16)) for i in range(2)]
        stg = [pen(nc.sbuf_tensor(f"stg{i}", [128, 2048], BF16)) for i in range(2)]
        s_x = pen(nc.sbuf_tensor("s_x", [128, 8 * 1024], F32))
        s_x0 = pen(nc.sbuf_tensor("s_x0", [128, 2 * 1024], F32))
        s_ncm = pen(nc.sbuf_tensor("s_ncm", [128, 2 * 8 * S], BF16))
        s_kv = pen(nc.sbuf_tensor("s_kv", [128, 8 * S], BF16))
        s_rr = pen(nc.sbuf_tensor("s_rr", [128, 8 * S], BF16))
        s_stt = pen(nc.sbuf_tensor("s_stt", [128, 8 * S], BF16))
        s_k05 = pen(nc.sbuf_tensor("s_k05", [128, 2 * S], BF16))
        s_vb = pen(nc.sbuf_tensor("s_vb", [128, 2 * S], BF16))
        s_xp = pen(nc.sbuf_tensor("s_xp", [128, 4 * 1024], BF16))
        s_wkvr = pen(nc.sbuf_tensor("s_wkvr", [128, 4 * 3 * 8 * 128], BF16))
        s_wo = pen(nc.sbuf_tensor("s_wo", [128, 8 * 1024], BF16))
        s_x2b = pen(nc.sbuf_tensor("s_x2b", [128, 4 * 1024], BF16))
        s_xcs = pen(nc.sbuf_tensor("s_xcs", [128, 8 * S], BF16))

        tasks = []

        def add_attn_tasks():
            wmap = [(0, wk, 0.05), (1, wv, 1.0), (2, wr, 1.0)]
            for L in range(NL):
                for wi, wt_, sc in wmap:
                    for oc in range(8):
                        def dst(e, stgbuf, L=L, wi=wi, oc=oc):
                            return e.dma_start(
                                out=wkvrT.ap()[L, oc, :, wi, :, :],
                                in_=stgbuf[:, 0:1024].rearrange("p (ic e) -> p ic e", ic=8))
                        tasks.append((wt_.ap()[L, oc * 128:(oc + 1) * 128, :], 8, dst,
                                      f"scr:kvr:{L}:{oc}:{wi}", (L, wi, sc, oc)))
                for oc in range(8):
                    def dsto(e, stgbuf, L=L, oc=oc):
                        return e.dma_start(
                            out=woT.ap()[L, :, :, oc * 128:(oc + 1) * 128],
                            in_=stgbuf[:, 0:1024].rearrange("p (c e) -> p c e", c=8))
                    tasks.append((wo.ap()[L, oc * 128:(oc + 1) * 128, :], 8, dsto,
                                  f"scr:wo:{L}:{oc}", None))

        def add_ffn_tasks():
            for wi, wt_ in ((0, w1), (1, w2)):
                for oc in range(32):
                    def dst12(e, stgbuf, wi=wi, oc=oc):
                        return e.dma_start(
                            out=w12T.ap()[oc, :, wi, :, :],
                            in_=stgbuf[:, 0:1024].rearrange("p (ic e) -> p ic e", ic=8))
                    tasks.append((wt_.ap()[oc * 128:(oc + 1) * 128, :], 8, dst12,
                                  f"scr:w12:{oc}:{wi}", None))
            for scr, wt_, no, nm in ((h1T, h1, 32, "h1"), (h2T, h2, 32, "h2"), (w3T, w3, 8, "w3")):
                for oc in range(no):
                    for half in range(2):
                        def dsth(e, stgbuf, scr=scr, oc=oc, half=half):
                            return e.dma_start(
                                out=scr.ap()[oc, :, half * 16:(half + 1) * 16, :],
                                in_=stgbuf[:, 0:2048].rearrange("p (ic e) -> p ic e", ic=16))
                        tasks.append((wt_.ap()[oc * 128:(oc + 1) * 128, half * 2048:(half + 1) * 2048],
                                      16, dsth, f"scr:{nm}:{oc}:{half}", None))

        pump_state = {"li": 0, "pi": 0}

        def pump_load():
            li = pump_state["li"]
            if li >= len(tasks) or li - pump_state["pi"] >= 2:
                return False
            src, nb, _, _, _ = tasks[li]
            i = li % 2
            em.dma("qp", lambda e, src=src, i=i, nb=nb: e.dma_start(
                out=pnl[i][:, 0:nb * 128], in_=src), writes=[f"pnl{i}"])
            pump_state["li"] = li + 1
            return True

        def pump_proc():
            pi = pump_state["pi"]
            if pi >= pump_state["li"]:
                return False
            src, nb, store_fn, skey, extra = tasks[pi]
            i = pi % 2
            kp, ks = f"pnl{i}", f"stg{i}"
            for g in range((nb + 7) // 8):
                lo, hi = g * 8, min(g * 8 + 8, nb)
                pt, ptk = (P_T, "P_T") if (pi + g) % 2 == 0 else (P_U, "P_U")
                def tr(e, i=i, lo=lo, hi=hi, pt=pt):
                    for c in range(lo, hi):
                        mm = e.matmul(pt[:, (c - lo) * 128:(c - lo + 1) * 128],
                                      pnl[i][:, c * 128:(c + 1) * 128], s_ident[:],
                                      is_transpose=True)
                    return mm
                em.emit("tensor", tr, reads=[kp, "ident"], writes=[ptk])
                em.emit("scalar", lambda e, i=i, lo=lo, hi=hi, pt=pt: e.copy(
                    stg[i][:, lo * 128:hi * 128], pt[:, 0:(hi - lo) * 128]),
                    reads=[ptk], writes=[ks])
            if extra is not None:
                L, wi, sc, oc = extra
                def bmm(e, i=i, L=L):
                    for ic in range(8):
                        mm = e.matmul(P_WO[:, 0:1], stg[i][:, ic * 128:(ic + 1) * 128],
                                      s_lnbb[:, L * 8 + ic:L * 8 + ic + 1],
                                      start=(ic == 0), stop=(ic == 7))
                    return mm
                em.emit("tensor", bmm, reads=[ks, "lnbb"], writes=["P_WO"])
                bslot = s_bias[:, (L * 3 + wi) * 8 + oc:(L * 3 + wi) * 8 + oc + 1]
                em.emit("scalar", lambda e, bslot=bslot, sc=sc: e.mul(bslot, P_WO[:, 0:1], sc),
                        reads=["P_WO"], writes=[f"bias:{L}:{wi}:{oc}"])
                for ic in range(8):
                    em.emit("vector", lambda e, i=i, ic=ic, L=L: e.tensor_scalar(
                        out=stg[i][:, ic * 128:(ic + 1) * 128], in0=stg[i][:, ic * 128:(ic + 1) * 128],
                        scalar1=s_lnw[:, L * 8 + ic:L * 8 + ic + 1], scalar2=None, op0=ALU.mult),
                        reads=[ks, "lnw"], writes=[ks])
            em.dma("qa", lambda e, i=i, store_fn=store_fn: store_fn(e, stg[i]), reads=[ks], writes=[skey])
            pump_state["pi"] = pi + 1
            return True

        def pump(n):
            for _ in range(n):
                a = pump_proc()
                b = pump_load()
                if not (a or b):
                    return

        add_attn_tasks()
        while pump_state["pi"] < 32:
            pump(1)
        add_ffn_tasks()
        pump(2)

        if debug == 1:
            while pump_state["pi"] < len(tasks):
                pump(1)
            em.final_waits()
            with nc.Block() as block:
                em.realize(block)
            return nc

        lncnt = [0]

        def ln_stats(xap, ukey):
            slot = lncnt[0] % 4
            lncnt[0] += 1
            st12 = s_stat[:, slot * 12:(slot + 1) * 12]
            mv = s_mv[:, slot * 2:(slot + 1) * 2]
            em.emit("vector", lambda e: e.bn_stats(st12[:, 0:6], xap[:, 0:512]),
                    reads=[ukey], writes=[f"stat{slot}"])
            em.emit("vector", lambda e: e.bn_stats(st12[:, 6:12], xap[:, 512:1024]),
                    reads=[ukey], writes=[f"stat{slot}"])
            em.emit("vector", lambda e: e.bn_aggr(mv, st12), reads=[f"stat{slot}"], writes=[f"mv{slot}"])
            em.emit("scalar", lambda e: e.activation(s_lnv[:, slot:slot + 1], mv[:, 1:2],
                                                     AF.Ln, bias=s_eps[:, 0:1], scale=1.0),
                    reads=[f"mv{slot}", "epsc"], writes=[f"lnv{slot}"])
            em.emit("scalar", lambda e: e.activation(s_rstd[:, slot:slot + 1], s_lnv[:, slot:slot + 1],
                                                     AF.Exp, scale=-0.5),
                    reads=[f"lnv{slot}"], writes=[f"rstd{slot}"])
            em.emit("scalar", lambda e: e.activation(s_nrstd[:, slot:slot + 1], s_rstd[:, slot:slot + 1],
                                                     AF.Identity, scale=-1.0),
                    reads=[f"rstd{slot}"], writes=[f"nrstd{slot}"])
            em.emit("scalar", lambda e: e.activation(s_nmr[:, slot:slot + 1], mv[:, 0:1],
                                                     AF.Identity, scale=s_nrstd[:, slot:slot + 1]),
                    reads=[f"mv{slot}", f"nrstd{slot}"], writes=[f"nmr{slot}"])
            return slot

        def ln_apply(out_ap, xap, slot, rkeys, wkeys):
            em.emit("scalar", lambda e: e.activation(
                out_ap, xap, AF.Identity,
                scale=s_rstd[:, slot:slot + 1], bias=s_nmr[:, slot:slot + 1]),
                reads=list(rkeys) + [f"nmr{slot}", f"rstd{slot}"], writes=list(wkeys))

        def stage_A_pre(st, L, i):
            gt = st * NSUB + i
            xs = gt % 8
            xap = s_x[:, xs * 1024:(xs + 1) * 1024]
            xkey = f"x:{xs}"
            slot = ln_stats(xap, xkey)
            xslot = i % 4
            xpap = s_xp[:, xslot * 1024:(xslot + 1) * 1024]
            ln_apply(xpap, xap, slot, [xkey], [f"xp{xslot}"])

        def stage_A_post(st, L, i):
            xslot = i % 4
            xpap = s_xp[:, xslot * 1024:(xslot + 1) * 1024]
            pt, ptk = (P_T, "P_T") if i % 2 == 0 else (P_U, "P_U")
            def trx(e, xpap=xpap, pt=pt):
                for c in range(8):
                    mm = e.matmul(pt[:, c * 128:(c + 1) * 128],
                                  xpap[:, c * 128:(c + 1) * 128], s_ident[:],
                                  is_transpose=True)
                return mm
            em.emit("tensor", trx, reads=[f"xp{xslot}", "ident"], writes=[ptk])
            ncm = s_ncm[:, (L % 2) * 8 * S:(L % 2 + 1) * 8 * S]
            if i % 2 == 0:
                em.emit("scalar", lambda e, i=i, ncm=ncm, pt=pt: e.copy(
                    ncm.rearrange("p (c t) -> p c t", c=8)[:, :, i * 128:(i + 1) * 128],
                    pt[:].rearrange("p (c t) -> p c t", c=8)),
                    reads=[ptk], writes=[f"ncm{L % 2}:{c}" for c in range(8)])
            else:
                em.emit("vector", lambda e, i=i, ncm=ncm, pt=pt: e.tensor_copy(
                    ncm.rearrange("p (c t) -> p c t", c=8)[:, :, i * 128:(i + 1) * 128],
                    pt[:].rearrange("p (c t) -> p c t", c=8)),
                    reads=[ptk], writes=[f"ncm{L % 2}:{c}" for c in range(8)])

        def stage_A_tile(st, L, i):
            stage_A_pre(st, L, i)
            stage_A_post(st, L, i)

        def stage_x_loads(st):
            for i in range(NSUB):
                gt = st * NSUB + i
                xs = gt % 8
                xap = s_x[:, xs * 1024:(xs + 1) * 1024]
                em.dma("qp", lambda e, gt=gt, xap=xap: e.dma_start(
                    out=xap, in_=x_in.ap()[gt * 128:(gt + 1) * 128, :]), writes=[f"x:{xs}"])

        stage_x_loads(0)
        stage_x_loads(1)
        for i in range(NSUB):
            stage_A_tile(0, 0, i)
        deferred = []

        for st in range(NST):
            for L in range(NL):
                pp = L % 2
                ncm = s_ncm[:, pp * 8 * S:(pp + 1) * 8 * S]
                nk = [f"ncm{pp}:{c}" for c in range(8)]
                for ec in range(8):
                    wb = (st * NL * 8 + L * 8 + ec) % 4
                    wkey = f"wkvr{wb}"
                    em.dma("qs", lambda e, L=L, ec=ec, wb=wb: e.dma_start(
                        out=s_wkvr[:, wb * 3072:(wb + 1) * 3072].rearrange("p (w ic e) -> p w ic e", w=3, ic=8),
                        in_=wkvrT.ap()[L, ec, :, :, :, :]),
                        reads=[f"scr:kvr:{L}:{ec}:{wi}" for wi in range(3)], writes=[wkey])
                    kb = s_bias[:, (L * 3 + 0) * 8 + ec:(L * 3 + 0) * 8 + ec + 1]
                    vb = s_bias[:, (L * 3 + 1) * 8 + ec:(L * 3 + 1) * 8 + ec + 1]
                    rb = s_bias[:, (L * 3 + 2) * 8 + ec:(L * 3 + 2) * 8 + ec + 1]
                    eb = ec % 2
                    k05 = s_k05[:, eb * S:(eb + 1) * S]
                    vbf = s_vb[:, eb * S:(eb + 1) * S]
                    evacs = [
                        lambda: em.emit("vector", lambda e, k05=k05, kb=kb: e.tensor_scalar(
                            out=k05, in0=P_K[:, 0:512], scalar1=0.05, scalar2=kb,
                            op0=ALU.mult, op1=ALU.add),
                            reads=["P_K", f"bias:{L}:0:{ec}"], writes=[f"k05{eb}"]),
                        lambda: em.emit("vector", lambda e, vbf=vbf, vb=vb: e.tensor_scalar(
                            out=vbf, in0=P_V[:, 0:512], scalar1=vb, scalar2=None, op0=ALU.add),
                            reads=["P_V", f"bias:{L}:1:{ec}"], writes=[f"vb{eb}"]),
                        lambda: em.emit("scalar", lambda e, ec=ec, rb=rb: e.activation(
                            s_rr[:, ec * S:(ec + 1) * S], P_WO[:, 0:512], AF.Sigmoid, scale=1.0, bias=rb),
                            reads=["P_WO", f"bias:{L}:2:{ec}"], writes=[f"rr:{ec}"]),
                    ]
                    for wi, (pw, pk) in enumerate(((P_K, "P_K"), (P_V, "P_V"), (P_WO[:, 0:512], "P_WO"))):
                        def kmm(e, wb=wb, wi=wi, pw=pw, ncm=ncm):
                            for dc in range(8):
                                mm = e.matmul(pw if wi == 2 else pw[:, 0:512],
                                              s_wkvr[:, wb * 3072 + (wi * 8 + dc) * 128: wb * 3072 + (wi * 8 + dc + 1) * 128],
                                              ncm[:, dc * S:(dc + 1) * S],
                                              start=(dc == 0), stop=(dc == 7))
                            return mm
                        em.emit("tensor", kmm, reads=[wkey] + nk, writes=[pk])
                        evacs[wi]()
                    em.emit("vector", lambda e, ec=ec, k05=k05, vbf=vbf: e.tensor_tensor(
                        s_kv[:, ec * S:(ec + 1) * S], k05, vbf, ALU.mult),
                        reads=[f"k05{eb}", f"vb{eb}"], writes=[f"kv:{ec}"])
                    cc = s_carry[:, L * 8 + ec:L * 8 + ec + 1]
                    em.emit("vector", lambda e, ec=ec, cc=cc: e.tensor_tensor_scan(
                        out=s_stt[:, ec * S:(ec + 1) * S], data0=s_decay[:], data1=s_kv[:, ec * S:(ec + 1) * S],
                        initial=cc, op0=ALU.mult, op1=ALU.add),
                        reads=[f"kv:{ec}", f"carry:{L}:{ec}", "decay"], writes=[f"st:{ec}"])
                    em.emit("vector", lambda e, ec=ec, cc=cc: e.tensor_copy(
                        cc, s_stt[:, ec * S + S - 1:ec * S + S]),
                        reads=[f"st:{ec}"], writes=[f"carry:{L}:{ec}"])
                    em.emit("vector", lambda e, ec=ec: e.tensor_tensor(
                        s_kv[:, ec * S:(ec + 1) * S], s_rr[:, ec * S:(ec + 1) * S],
                        s_stt[:, ec * S:(ec + 1) * S], ALU.mult),
                        reads=[f"rr:{ec}", f"st:{ec}"], writes=[f"kv:{ec}"])
                    pump(3 if st == 0 else 1)
                em.dma("qs", lambda e, L=L: e.dma_start(
                    out=s_wo[:], in_=woT.ap()[L, :, :, :]),
                    reads=[f"scr:wo:{L}:{oc}" for oc in range(8)], writes=["wo"])
                def nextA_pre(i):
                    if L == 0:
                        stage_A_pre(st, 1, i)
                    elif st + 1 < NST:
                        stage_A_pre(st + 1, 0, i)
                def nextA_post(i):
                    if L == 0:
                        stage_A_post(st, 1, i)
                    elif st + 1 < NST:
                        stage_A_post(st + 1, 0, i)
                for i in range(NSUB):
                    gt = st * NSUB + i
                    xs = gt % 8
                    xap = s_x[:, xs * 1024:(xs + 1) * 1024]
                    pwo, pwok = (P_WO, "P_WO") if i % 2 == 0 else (P_WO2, "P_WO2")
                    for c in range(8):
                        def wom(e, i=i, c=c, pwo=pwo):
                            for dh in range(2):
                                mm = e.matmul(pwo[:, dh * 512:(dh + 1) * 512],
                                              s_kv[:, c * S + i * 128:c * S + (i + 1) * 128],
                                              s_wo[:, c * 1024 + dh * 512:c * 1024 + (dh + 1) * 512],
                                              start=(c == 0), stop=(c == 7))
                            return mm
                        em.emit("tensor", wom, reads=[f"kv:{c}", "wo"], writes=[pwok],
                                same_engine_ok=(c > 0))
                    em.emit("vector", lambda e, xap=xap, pwo=pwo: e.tensor_tensor(
                        xap, xap, pwo[:, 0:1024], ALU.add),
                        reads=[pwok, f"x:{xs}"], writes=[f"x:{xs}"])
                    nextA_pre(i)
                    if L == 0:
                        for _ in range(2):
                            if deferred:
                                deferred.pop(0)()
                    if i >= 2:
                        nextA_post(i - 2)
                nextA_post(NSUB - 2)
                nextA_post(NSUB - 1)
            def make_postln(st):
                def chain(i):
                    gt = st * NSUB + i
                    xs = gt % 8
                    xslot = i % 2
                    bslot = i % 4
                    xap = s_x[:, xs * 1024:(xs + 1) * 1024]
                    x0ap = s_x0[:, xslot * 1024:(xslot + 1) * 1024]
                    em.dma("qp", lambda e, gt=gt, x0ap=x0ap: e.dma_start(
                        out=x0ap, in_=x_in.ap()[gt * 128:(gt + 1) * 128, :]), writes=[f"x0{xslot}"])
                    em.emit("vector", lambda e, xap=xap, x0ap=x0ap: e.tensor_tensor(
                        xap, xap, x0ap, ALU.add), reads=[f"x:{xs}", f"x0{xslot}"], writes=[f"x:{xs}"])
                    slot = ln_stats(xap, f"x:{xs}")
                    ln_apply(xap, xap, slot, [f"x:{xs}"], [f"x:{xs}"])
                    em.dma("qa", lambda e, gt=gt, xap=xap: e.dma_start(
                        out=x2f.ap()[gt * 128:(gt + 1) * 128, :], in_=xap),
                        reads=[f"x:{xs}"], writes=[f"scr:x2f:{gt}"])
                    x2b = s_x2b[:, bslot * 1024:(bslot + 1) * 1024]
                    em.emit("vector", lambda e, xap=xap, x2b=x2b: e.tensor_copy(x2b, xap),
                            reads=[f"x:{xs}"], writes=[f"x2b{bslot}"])
                def tpart(i):
                    bslot = i % 4
                    x2b = s_x2b[:, bslot * 1024:(bslot + 1) * 1024]
                    pt, ptk = (P_T, "P_T") if i % 2 == 0 else (P_U, "P_U")
                    def trx2(e, x2b=x2b, pt=pt):
                        for c in range(8):
                            mm = e.matmul(pt[:, c * 128:(c + 1) * 128],
                                          x2b[:, c * 128:(c + 1) * 128], s_ident[:], is_transpose=True)
                        return mm
                    em.emit("tensor", trx2, reads=[f"x2b{bslot}", "ident"], writes=[ptk])
                    em.emit("scalar", lambda e, i=i, pt=pt: e.copy(
                        s_xcs[:].rearrange("p (c t) -> p c t", c=8)[:, :, i * 128:(i + 1) * 128],
                        pt[:].rearrange("p (c t) -> p c t", c=8)),
                        reads=[ptk], writes=["xcs"])
                def fin():
                    em.dma("qa", lambda e, st=st: e.dma_start(
                        out=x2cm.ap()[st, :, :, :],
                        in_=s_xcs[:].rearrange("p (c t) -> p c t", c=8)),
                        reads=["xcs"], writes=[f"scr:x2cm:{st}"])
                return [lambda: chain(0), lambda: chain(1),
                        lambda: (chain(2), tpart(0)), lambda: (chain(3), tpart(1)),
                        lambda: tpart(2), lambda: tpart(3), fin]
            deferred.extend(make_postln(st))
            if st + 2 < NST:
                deferred.append(lambda st=st: stage_x_loads(st + 2))
            if st == NST - 1:
                while deferred:
                    deferred.pop(0)()
            pump(2)
        while pump_state["pi"] < len(tasks):
            pump(1)
        em.barrier()

        if debug == 2:
            em.final_waits()
            with nc.Block() as block:
                em.realize(block)
            return nc

    with ExitStack() as p2:
        pen = p2.enter_context
        s_xcm = pen(nc.sbuf_tensor("s_xcm", [128, 2 * 8 * S], BF16))
        s_h = pen(nc.sbuf_tensor("s_h", [128, 32 * S], BF16))
        s_hh = pen(nc.sbuf_tensor("s_hh", [128, 32 * S], BF16))
        s_w12 = pen(nc.sbuf_tensor("s_w12", [128, 2 * 2048], BF16))
        s_hw = pen(nc.sbuf_tensor("s_hw", [128, 3 * 4096], BF16))
        s_sa = pen(nc.sbuf_tensor("s_sa", [128, 2 * S], BF16))
        s_sb = pen(nc.sbuf_tensor("s_sb", [128, 2 * S], BF16))
        s_fcm = pen(nc.sbuf_tensor("s_fcm", [128, 8 * S], BF16))
        s_x2r = pen(nc.sbuf_tensor("s_x2r", [128, 2 * 1024], F32))
        s_u = pen(nc.sbuf_tensor("s_u", [128, 2 * 1024], F32))
        s_outb = pen(nc.sbuf_tensor("s_outb", [128, 2 * 1024], F32))

        psl = [(P_K[:, 0:512], "P_K"), (P_V[:, 0:512], "P_V"),
               (P_WO[:, 0:512], "P_WOa"), (P_WO[:, 512:1024], "P_WOb"),
               (P_WO2[:, 0:512], "P_WO2a"), (P_WO2[:, 512:1024], "P_WO2b")]
        pscnt = [0]

        def pslot():
            p, k = psl[pscnt[0] % 6]
            pscnt[0] += 1
            return p, k

        lncnt2 = [0]

        def ln_stats2(xap, ukey):
            slot = lncnt2[0] % 4
            lncnt2[0] += 1
            st12 = s_stat[:, slot * 12:(slot + 1) * 12]
            mv = s_mv[:, slot * 2:(slot + 1) * 2]
            em.emit("vector", lambda e: e.bn_stats(st12[:, 0:6], xap[:, 0:512]),
                    reads=[ukey], writes=[f"stat{slot}"])
            em.emit("vector", lambda e: e.bn_stats(st12[:, 6:12], xap[:, 512:1024]),
                    reads=[ukey], writes=[f"stat{slot}"])
            em.emit("vector", lambda e: e.bn_aggr(mv, st12), reads=[f"stat{slot}"], writes=[f"mv{slot}"])
            em.emit("scalar", lambda e: e.activation(s_lnv[:, slot:slot + 1], mv[:, 1:2],
                                                     AF.Ln, bias=s_eps[:, 0:1], scale=1.0),
                    reads=[f"mv{slot}", "epsc"], writes=[f"lnv{slot}"])
            em.emit("scalar", lambda e: e.activation(s_rstd[:, slot:slot + 1], s_lnv[:, slot:slot + 1],
                                                     AF.Exp, scale=-0.5),
                    reads=[f"lnv{slot}"], writes=[f"rstd{slot}"])
            em.emit("scalar", lambda e: e.activation(s_nrstd[:, slot:slot + 1], s_rstd[:, slot:slot + 1],
                                                     AF.Identity, scale=-1.0),
                    reads=[f"rstd{slot}"], writes=[f"nrstd{slot}"])
            em.emit("scalar", lambda e: e.activation(s_nmr[:, slot:slot + 1], mv[:, 0:1],
                                                     AF.Identity, scale=s_nrstd[:, slot:slot + 1]),
                    reads=[f"mv{slot}", f"nrstd{slot}"], writes=[f"nmr{slot}"])
            return slot

        def ln_apply2(out_ap, xap, slot, rkeys, wkeys):
            em.emit("scalar", lambda e: e.activation(
                out_ap, xap, AF.Identity,
                scale=s_rstd[:, slot:slot + 1], bias=s_nmr[:, slot:slot + 1]),
                reads=list(rkeys) + [f"nmr{slot}", f"rstd{slot}"], writes=list(wkeys))

        for st in range(NST):
            xb = st % 2
            xcmk = f"xcm{xb}"
            xcm = s_xcm[:, xb * 8 * S:(xb + 1) * 8 * S]
            em.dma("qp", lambda e, st=st, xcm=xcm: e.dma_start(
                out=xcm.rearrange("p (c t) -> p c t", c=8),
                in_=x2cm.ap()[st, :, :, :]),
                reads=[f"scr:x2cm:{st}"], writes=[xcmk])
            for oc in range(32):
                wb = oc % 2
                em.dma("qs", lambda e, oc=oc, wb=wb: e.dma_start(
                    out=s_w12[:, wb * 2048:(wb + 1) * 2048].rearrange("p (w ic e) -> p w ic e", w=2, ic=8),
                    in_=w12T.ap()[oc, :, :, :, :]),
                    reads=[f"scr:w12:{oc}:0", f"scr:w12:{oc}:1"], writes=[f"w12{wb}"])
                pa, pak = pslot()
                pb, pbk = pslot()
                def mma(e, wb=wb, pa=pa, xcm=xcm):
                    for ic in range(8):
                        mm = e.matmul(pa, s_w12[:, wb * 2048 + ic * 128:wb * 2048 + (ic + 1) * 128],
                                      xcm[:, ic * S:(ic + 1) * S], start=(ic == 0), stop=(ic == 7))
                    return mm
                def mmb(e, wb=wb, pb=pb, xcm=xcm):
                    for ic in range(8):
                        mm = e.matmul(pb, s_w12[:, wb * 2048 + (8 + ic) * 128:wb * 2048 + (9 + ic) * 128],
                                      xcm[:, ic * S:(ic + 1) * S], start=(ic == 0), stop=(ic == 7))
                    return mm
                def mmab(e, mma=mma, mmb=mmb):
                    mma(e)
                    return mmb(e)
                em.emit("tensor", mmab, reads=[f"w12{wb}", xcmk], writes=[pak, pbk])
                sa = s_sa[:, wb * S:(wb + 1) * S]
                sb = s_sb[:, wb * S:(wb + 1) * S]
                em.emit("scalar", lambda e, sa=sa, pa=pa, oc=oc: e.activation(
                    sa, pa, AF.Silu, scale=s_s1[:, oc:oc + 1]),
                    reads=[pak, "s1"], writes=[f"sa{wb}"])
                em.emit("scalar", lambda e, sb=sb, pb=pb, oc=oc: e.activation(
                    sb, pb, AF.Identity, scale=s_s2[:, oc:oc + 1]),
                    reads=[pbk, "s2"], writes=[f"sb{wb}"])
                em.emit("vector", lambda e, oc=oc, sa=sa, sb=sb: e.tensor_tensor(
                    s_h[:, oc * S:(oc + 1) * S], sa, sb, ALU.mult),
                    reads=[f"sa{wb}", f"sb{wb}"], writes=[f"h:{oc}"])
            hkeys = [f"h:{c}" for c in range(32)]
            for oc in range(32):
                wb = oc % 3
                em.dma("qs", lambda e, oc=oc, wb=wb: e.dma_start(
                    out=s_hw[:, wb * 4096:(wb + 1) * 4096].rearrange("p (ic e) -> p ic e", ic=32),
                    in_=h1T.ap()[oc, :, :, :]),
                    reads=[f"scr:h1:{oc}:0", f"scr:h1:{oc}:1"], writes=[f"hw{wb}"])
                pa, pak = pslot()
                def hmm(e, wb=wb, pa=pa):
                    for ic in range(32):
                        mm = e.matmul(pa, s_hw[:, wb * 4096 + ic * 128:wb * 4096 + (ic + 1) * 128],
                                      s_h[:, ic * S:(ic + 1) * S], start=(ic == 0), stop=(ic == 31))
                    return mm
                em.emit("tensor", hmm, reads=[f"hw{wb}"] + hkeys, writes=[pak])
                em.emit("scalar", lambda e, oc=oc, pa=pa: e.activation(
                    s_hh[:, oc * S:(oc + 1) * S], pa, AF.Gelu, scale=s_h1s[:, oc:oc + 1]),
                    reads=[pak, "h1s"], writes=[f"hh:{oc}"])
            hhkeys = [f"hh:{c}" for c in range(32)]
            for oc in range(32):
                wb = oc % 3
                em.dma("qs", lambda e, oc=oc, wb=wb: e.dma_start(
                    out=s_hw[:, wb * 4096:(wb + 1) * 4096].rearrange("p (ic e) -> p ic e", ic=32),
                    in_=h2T.ap()[oc, :, :, :]),
                    reads=[f"scr:h2:{oc}:0", f"scr:h2:{oc}:1"], writes=[f"hw{wb}"])
                pa, pak = pslot()
                def h2mm(e, wb=wb, pa=pa):
                    for ic in range(32):
                        mm = e.matmul(pa, s_hw[:, wb * 4096 + ic * 128:wb * 4096 + (ic + 1) * 128],
                                      s_hh[:, ic * S:(ic + 1) * S], start=(ic == 0), stop=(ic == 31))
                    return mm
                em.emit("tensor", h2mm, reads=[f"hw{wb}"] + hhkeys, writes=[pak])
                em.emit("scalar", lambda e, oc=oc, pa=pa: e.activation(
                    s_h[:, oc * S:(oc + 1) * S], pa, AF.Gelu, scale=s_h2s[:, oc:oc + 1]),
                    reads=[pak, "h2s"], writes=[f"h:{oc}"])
            for oc in range(8):
                wb = oc % 3
                em.dma("qs", lambda e, oc=oc, wb=wb: e.dma_start(
                    out=s_hw[:, wb * 4096:(wb + 1) * 4096].rearrange("p (ic e) -> p ic e", ic=32),
                    in_=w3T.ap()[oc, :, :, :]),
                    reads=[f"scr:w3:{oc}:0", f"scr:w3:{oc}:1"], writes=[f"hw{wb}"])
                pa, pak = pslot()
                def w3mm(e, wb=wb, pa=pa):
                    for ic in range(32):
                        mm = e.matmul(pa, s_hw[:, wb * 4096 + ic * 128:wb * 4096 + (ic + 1) * 128],
                                      s_h[:, ic * S:(ic + 1) * S], start=(ic == 0), stop=(ic == 31))
                    return mm
                em.emit("tensor", w3mm, reads=[f"hw{wb}"] + hkeys, writes=[pak])
                em.emit("scalar", lambda e, oc=oc, pa=pa: e.activation(
                    s_fcm[:, oc * S:(oc + 1) * S], pa, AF.Identity, scale=s_w3s[:, oc:oc + 1]),
                    reads=[pak, "w3s"], writes=[f"fcm:{oc}"])
            for i in range(NSUB):
                gt = st * NSUB + i
                slot = i % 2
                x2r = s_x2r[:, slot * 1024:(slot + 1) * 1024]
                em.dma("qp", lambda e, gt=gt, x2r=x2r: e.dma_start(
                    out=x2r, in_=x2f.ap()[gt * 128:(gt + 1) * 128, :]),
                    reads=[f"scr:x2f:{gt}"], writes=[f"x2r{slot}"])
                pt, ptk = (P_T, "P_T") if i % 2 == 0 else (P_U, "P_U")
                def trf(e, i=i, pt=pt):
                    for c in range(8):
                        mm = e.matmul(pt[:, c * 128:(c + 1) * 128],
                                      s_fcm[:, c * S + i * 128:c * S + (i + 1) * 128],
                                      s_ident[:], is_transpose=True)
                    return mm
                em.emit("tensor", trf, reads=[f"fcm:{c}" for c in range(8)] + ["ident"], writes=[ptk])
                uap = s_u[:, slot * 1024:(slot + 1) * 1024]
                em.emit("vector", lambda e, uap=uap, x2r=x2r, pt=pt: e.tensor_tensor(
                    uap, x2r, pt[:], ALU.add),
                    reads=[ptk, f"x2r{slot}"], writes=[f"u{slot}"])
                lslot = ln_stats2(uap, f"u{slot}")
                ob = s_outb[:, slot * 1024:(slot + 1) * 1024]
                ln_apply2(ob, uap, lslot, [f"u{slot}"], [f"ob{slot}"])
                em.dma("qa", lambda e, gt=gt, ob=ob: e.dma_start(
                    out=y_out.ap()[gt * 128:(gt + 1) * 128, :], in_=ob), reads=[f"ob{slot}"])

    em.final_waits()
    with nc.Block() as block:
        em.realize(block)
    stack.close()
    return nc


def kernel(L0_attn_h, L0_attn_ln_w, L0_attn_ln_b, L0_attn_Wk, L0_attn_Wv, L0_attn_Wr,
           L0_attn_Wo, L0_ffn_w1_w, L0_ffn_w1_s, L0_ffn_w2_w, L0_ffn_w2_s,
           L0_ffn_h1_w, L0_ffn_h1_s, L0_ffn_h2_w, L0_ffn_h2_s, L0_ffn_w3_w, L0_ffn_w3_s):
    x = np.ascontiguousarray(np.asarray(L0_attn_h, dtype=np.float32))
    f32 = np.float32
    c = np.ascontiguousarray
    lnw_cv = c(np.asarray(L0_attn_ln_w, f32).reshape(NL, 8, 128).transpose(0, 2, 1))
    lnb_cv = c(np.asarray(L0_attn_ln_b, f32).reshape(NL, 8, 128).transpose(0, 2, 1))
    s1cv = c(np.asarray(L0_ffn_w1_s, f32).reshape(32, 128).T)
    s2cv = c(np.asarray(L0_ffn_w2_s, f32).reshape(32, 128).T)
    h1scv = c(np.asarray(L0_ffn_h1_s, f32).reshape(32, 128).T)
    h2scv = c(np.asarray(L0_ffn_h2_s, f32).reshape(32, 128).T)
    w3scv = c(np.asarray(L0_ffn_w3_s, f32).reshape(8, 128).T)
    ident = np.eye(128, dtype=np.float32).astype(bfdt)

    base = dict(
        wk=c(np.asarray(L0_attn_Wk, f32)), wv=c(np.asarray(L0_attn_Wv, f32)),
        wr=c(np.asarray(L0_attn_Wr, f32)), wo=c(np.asarray(L0_attn_Wo, f32)),
        lnw_c=lnw_cv, lnb_c=lnb_cv,
        w1=c(np.asarray(L0_ffn_w1_w, f32)), w2=c(np.asarray(L0_ffn_w2_w, f32)),
        h1=c(np.asarray(L0_ffn_h1_w, f32)), h2=c(np.asarray(L0_ffn_h2_w, f32)),
        w3=c(np.asarray(L0_ffn_w3_w, f32)),
        s1c=s1cv, s2c=s2cv, h1sc=h1scv, h2sc=h2scv, w3sc=w3scv, ident_in=ident,
    )
    in_maps = []
    for b in range(NCORES):
        m = dict(base)
        m["x_in"] = c(x[b])
        in_maps.append(m)

    nc = build_program()
    res = run_bass_kernel_spmd(nc, in_maps, core_ids=list(range(NCORES)))
    out = np.stack([np.asarray(res.results[b]["y"], dtype=np.float32) for b in range(NCORES)], axis=0)
    return out
